# version 01 (final)
# baseline (speedup 1.0000x reference)
import numpy as np
import concourse.bass as bass
import concourse.mybir as mybir
from concourse.bass_utils import run_bass_kernel_spmd
from concourse.bass import IndirectOffsetOnAxis

F32 = mybir.dt.float32
BF16 = mybir.dt.bfloat16
I32 = mybir.dt.int32
U32 = mybir.dt.uint32
AF = mybir.ActivationFunctionType
ALU = mybir.AluOpType
AXX = mybir.AxisListType.X

C32 = float(1.5 * 2 ** 23)
C52 = float(1.5 * 2 ** 52)
D = 2048
T = 2048
DI = 8192
NT = 16
NSLOT = 16384
NCHK = 128

ENGS = ("sync", "vector", "scalar", "tensor", "gpsimd")

CR_NQ = 0
CR_E6 = 1
CR_E5 = 2
CR_L127 = 3
CR_HALF = 4
CR_ONEC = 5
CR_L8 = 6

RW_L, RW_LB, RW_B, RW_EB, RW_EXL, RW_RQEB, RW_RKEB, RW_RKEXL, RW_RQ, RW_RK \
    = range(10)
NROWS = 10


NQSEM = 6


class Plan:
    def __init__(self):
        self.prog = {e: [] for e in ENGS}
        self.sem_names = (["eT", "eV", "eS", "eG"]
                          + [f"{q}{i}" for q in ("qS", "qA", "qG")
                             for i in range(NQSEM)])
        self.cnt = {s: 0 for s in self.sem_names}
        self.qidx = {"qS": 0, "qA": 0, "qG": 0}
        self.esem = {"vector": "eV", "scalar": "eS", "tensor": "eT",
                     "gpsimd": "eG"}
        self.qsem = {"sync": "qS", "scalar": "qA", "gpsimd": "qG"}
        self.sems = None

    def op(self, eng, fn):
        s = self.esem[eng]
        if eng != "tensor":
            self.wait(eng, (s, self.cnt[s]))
        self.cnt[s] += 1

        def cl(e, fn=fn, s=s):
            fn(e).then_inc(self.sems[s], 1)
        self.prog[eng].append(cl)
        return (s, self.cnt[s])

    def _qslot(self, eng):
        q = self.qsem[eng]
        i = self.qidx[q]
        self.qidx[q] = i + 1
        s = f"{q}{i % NQSEM}"
        self.wait(eng, (s, self.cnt[s]))
        self.cnt[s] += 16
        return s

    def dma(self, eng, out, in_):
        import numpy as _np
        assert _np.prod(out.shape) == _np.prod(in_.shape), \
            (out.shape, in_.shape)
        s = self._qslot(eng)

        def cl(e, out=out, in_=in_, s=s):
            e.dma_start(out=out, in_=in_).then_inc(self.sems[s], 16)
        self.prog[eng].append(cl)
        return (s, self.cnt[s])

    def gather(self, out, in_, idx_ap, element_offset=0):
        s = self._qslot("gpsimd")

        def cl(e, out=out, in_=in_, idx=idx_ap, eo=element_offset, s=s):
            e.indirect_dma_start(
                out=out, out_offset=None, in_=in_,
                in_offset=IndirectOffsetOnAxis(idx, 0),
                element_offset=eo).then_inc(self.sems[s], 16)
        self.prog["gpsimd"].append(cl)
        return (s, self.cnt[s])

    def wait(self, eng, token):
        if token is None:
            return
        s, v = token
        if v <= 0:
            return

        def cl(e, s=s, v=v):
            e.wait_ge(self.sems[s], v)
        self.prog[eng].append(cl)

    def fence(self, eng):
        self.wait(eng, (self.esem[eng], self.cnt[self.esem[eng]]))

    def wait_cur(self, eng, *sems):
        for s in sems:
            if s in ("qS", "qA", "qG"):
                for i in range(NQSEM):
                    self.wait(eng, (f"{s}{i}", self.cnt[f"{s}{i}"]))
            else:
                self.wait(eng, (s, self.cnt[s]))

    def barrier(self):
        snap = dict(self.cnt)
        for e in ENGS:
            for s, v in snap.items():
                if v == 0:
                    continue
                self.wait(e, (s, v))


def build(meta, upto=8, dump=None):
    nc = bass.Bass("TRN2", target_bir_lowering=False, debug=False,
                   detect_race_conditions=False)
    p = Plan()
    WSEG = meta["WSEG"]

    def din(name, shape, dt=F32):
        return nc.dram_tensor(name, list(shape), dt, kind="ExternalInput").ap()

    def dint(name, shape, dt=F32):
        return nc.dram_tensor(name, list(shape), dt, kind="Internal").ap()

    x_in = din("x", [T, D])
    W_in_d = din("W_in", [D, 384])
    W_out0_d = din("W_out0", [128, D])
    W_skip_d = din("W_skip", [D, D])
    w1_d = din("w1", [DI, D])
    w2_d = din("w2", [D, DI])
    W_qkv_d = din("W_qkv", [D, D])
    W_z_d = din("W_z", [1024, D])
    Wb_d = din("Wb", [16, D])
    Wa_d = din("Wa", [16, D])
    W_out1_d = din("W_out1", [D, 1024])
    conv_d = din("conv_w", [128, 16, 4])
    dtal_d = din("dtb_alog", [16, 2])
    gate_w_d = din("gate_w", [8, D])
    Wffn_d = din("Wffn", [D, D])
    atoms_d = din("atoms_s", [64, NSLOT])
    imp_d = din("imp_s", [8, NSLOT])
    eaw_d = din("eaw", [8, 64])
    identf_d = din("identf", [128, 128])
    identb_d = din("identb", [128, 128], BF16)
    eh128_d = din("eh128", [16, 16 * 128])
    ehn128_d = din("ehn128", [16, 16 * 128])
    eh64_d = din("eh64", [16, 16 * 64])
    esq_d = din("esq", [128, 64])
    mUneg_d = din("maskUneg", [128, 128])
    mLneg_d = din("maskLneg", [128, 128])
    mUinc_d = din("maskUinc", [128, 128])
    crow_d = din("crow", [128, 8])
    crow16_d = din("crow16", [16, 8])
    onescol_d = din("onescol", [128, 1])
    onesrow_d = din("onesrow", [1, 128])
    allones_d = din("allones", [128, 128])
    id64_d = din("id64", [64, 64])
    gidx_d = din("gidx", [128, NCHK], I32)
    seg_d = din("seg", [NCHK * 128, WSEG])
    negbig_d = din("negbig", [128, 8])

    h_scr = dint("h_scr", [NT, 128, DI])
    wq2T_scr = dint("wq2T_scr", [DI, D], BF16)
    g_scr = dint("g_scr", [T, D])
    x1_scr = dint("x1_scr", [T, D])
    xn1T_scr = dint("xn1T_scr", [D, T])
    z_scr = dint("z_scr", [T, 1024])
    q_scr = dint("q_scr", [512, T])
    k_scr = dint("k_scr", [512, T])
    v_scr = dint("v_scr", [1024, T])
    wqkvT_scr = dint("wqkvT_scr", [D, D])
    wzT_scr = dint("wzT_scr", [D, 1024])
    wT_scr = dint("wT_scr", [D, D])
    woutT_scr = dint("woutT_scr", [1024, D])
    rows_scr = dint("rows_scr", [16, NROWS, T])
    pt_scr = dint("pt_scr", [8, T])
    wp_scr = dint("wp_scr", [8, NSLOT])
    out_d = nc.dram_tensor("out", [T, D], F32, kind="ExternalOutput").ap()

    def st(name, shape, dt=F32):
        return nc.alloc_sbuf_tensor(name, list(shape), dt).ap()

    slabA = st("slabA", [128, 16384])
    xqT = slabA.bitcast(BF16).rearrange("p (k t) -> p k t", k=16)
    xq2T = slabA.bitcast(BF16).rearrange("p (k t) -> p k t", k=64)
    wqkvbuf = slabA[:, 0:4096].rearrange("p (a k c) -> p a k c", a=2, k=16)
    ynzT = slabA.rearrange("p (k t) -> p k t", k=8)

    slabB = st("slabB", [128, 12288])
    winsb = slabB[:, 0:6144].rearrange("p (k e) -> p k e", k=16)
    scan_oma = slabB[:, 6144:8192]
    scan_ab = slabB[:, 8192:10240]
    scan_tc = slabB[:, 10240:12288]
    wq1T = slabB[:, 0:4096].bitcast(BF16).rearrange(
        "p (k h) -> p k h", k=16)
    hact4 = slabB[:, 4096:6144].rearrange("p (a h) -> p a h", a=4)
    xnTst = slabB[:, 0:8192].rearrange("p (k t) -> p k t", k=16)
    x1TBbuf = slabB[:, 0:8192].rearrange("p (k t) -> p k t", k=16)
    eh128_s = slabB[0:16, 0:2048]
    ehn128_s = slabB[0:16, 2048:4096]
    eh64_s = slabB[0:16, 4096:5120]
    rowchk = slabB[0:16, 5120:6400].rearrange("q (r c) -> q r c", r=NROWS)
    qc_sb = slabB[0:64, 6400:7424].rearrange("p (a c) -> p a c", a=8)
    kc_sb = slabB[0:64, 7424:8448].rearrange("p (a c) -> p a c", a=8)
    vc_sb = slabB[0:64, 8448:10496].rearrange("p (a c) -> p a c", a=16)
    zc_sb = slabB[:, 10496:11520]
    wTblk = slabB[:, 0:8192].rearrange("p (k o) -> p k o", k=16)
    woutTblk = slabB[:, 8192:12288].rearrange("p (k o) -> p k o", k=8)

    slabC = st("slabC", [128, 8192])
    wsq_tC = slabC[:, 0:2048]
    hstat = slabC[:, 0:4096].rearrange("p (a b) -> p a b", a=4)
    wout0_s = slabC[:, 4096:6144]
    w2stage = slabC[:, 6144:6400].bitcast(BF16)
    w2t2 = slabC[:, 6656:7168].bitcast(BF16).rearrange(
        "p (a h) -> p a h", a=2)
    rsqq_s = slabC[0:16, 0:2048]
    rsqk_s = slabC[0:16, 2048:4096]
    wzTh = slabC[:, 4096:8192].rearrange("p (k h) -> p k h", k=8)
    contribTB = slabC[:, 0:2048].rearrange("p (a o) -> p a o", a=4)
    xn1Ttile = slabC[:, 2048:6144].rearrange("p (a kt) -> p a kt", a=2)

    work = st("work", [128, 4096])
    wsq_t = wsq_tC
    ynstage = work[:, 0:1024]
    wA = work[:, 0:1024]
    wB = work[:, 1024:2048]
    wC = work[:, 2048:3072]
    wD = work[:, 3072:4096]

    xw0 = st("xw0", [128, D])
    xw1 = st("xw1", [128, D])
    xn_t = st("xn_t", [128, D])
    qbf = st("qbf", [128, 1024], BF16)
    yfm = st("yfm", [128, T])
    gat = st("gat", [128, 384])
    xnT2 = st("xnT2", [128, 2, 128])

    identf_s = st("identf_s", [128, 128])
    identb_s = st("identb_s", [128, 128], BF16)
    esq_s = st("esq_s", [128, 64])
    mUneg = st("mUneg", [128, 128])
    mLneg = st("mLneg", [128, 128])
    mUinc = st("mUinc", [128, 128])
    crow_s = st("crow_s", [128, 8])
    crow16_s = st("crow16_s", [16, 8])
    ones_s = st("ones_s", [128, 1])
    allones_s = st("allones_s", [128, 128])
    id64_s = st("id64_s", [64, 64])
    negbig_s = st("negbig_s", [128, 8])
    gidx_s = st("gidx_s", [128, NCHK], I32)
    conv_s = st("conv_s", [128, 16, 4])
    dtal_s = st("dtal_s", [16, 2])
    gwT_s = st("gwT_s", [128, 16, 8])
    wincs = st("wincs", [1, 384])
    wabT_s = st("wabT_s", [128, 16, 32])

    stats = st("stats", [128, 48])
    SM_MU0, SM_RS0, SM_NM0 = 0, 16, 32
    statc = st("statc", [128, 40])
    hsc_s = st("hsc_s", [128, NT])
    hbc_s = st("hbc_s", [128, NT])
    comb1_s = st("comb1_s", [128, NT])
    comb2_s = st("comb2_s", [128, NT])
    qsc_s = st("qsc_s", [128, NT])
    invq_s = st("invq_s", [128, NT])
    rstd2_s = st("rstd2_s", [128, NT])
    nmr2_s = st("nmr2_s", [128, NT])
    s1sum = st("s1sum", [128, NT])
    s1ss = st("s1ss", [128, NT])
    rstd1_s = st("rstd1_s", [128, NT])
    nmr1_s = st("nmr1_s", [128, NT])
    mu1_s = st("mu1_s", [128, NT])
    swcol = st("swcol", [128, 4])
    strow = st("strow", [2, 128])
    stc2 = st("stc2", [128, 2])
    acc64 = st("acc64", [128, 64])

    E1 = slabC[:, 0:128]
    E1b = slabC[:, 128:256]
    E2 = slabC[:, 256:384]
    Gbar = slabC[:, 384:512]
    Gqbar = slabC[:, 512:640]
    P0b = slabC[:, 640:768]
    N0b = slabC[:, 768:896]
    Pk = slabC[:, 896:1024]
    Nk = slabC[:, 1024:1152]
    AqT = slabC[:, 1152:1280]
    Xa = slabC[:, 1280:1344]
    Xb = slabC[:, 1344:1408]
    Vt = slabC[:, 1408:1472]
    Kt = slabC[:, 1472:1536]
    Rt = slabC[:, 1536:1600]
    osb = slabC[:, 1600:1664]
    osq = slabC[:, 1664:1728]
    colsL = slabC[:, 1728:1744]
    colsLB = slabC[:, 1744:1760]
    colsB = slabC[:, 1760:1776]
    colsRQE = slabC[:, 1776:1792]
    colsRKE = slabC[:, 1792:1808]
    colsRKX = slabC[:, 1808:1824]
    colsRK = slabC[:, 1824:1840]
    Ssta = slabC[0:64, 1840:2864].rearrange("p (a c) -> p a c", a=16, c=64)
    bCI = slabC[0:64, 2864:2928]
    bCc = slabC[0:64, 2928:2929]
    rkb = slabC[:, 2944:3072]
    rqb = slabC[:, 3072:3200]
    aqb = slabC[:, 3200:3264]
    sscol = st("sscol", [128, 1])
    rsqc = st("rsqc", [128, 1])

    rowt = slabA[0:16, 9216:11264]
    rt_ = slabA[0:16, 4096:4096 + NROWS * 512].rearrange(
        "q (r c) -> q r c", r=NROWS)
    ones16 = st("ones16", [16, 128])
    negea = st("negea", [16, 1])

    xgT = slabC[:, 6144:6656]
    prodb = slabC[:, 6656:7168]
    t512a = slabC[:, 7168:7680]
    t512b = slabC[:, 7680:8192]
    x1sl = work[:, 0:512]
    g1sl = work[:, 512:1024]
    logit = st("logit", [128, 8])
    Ptile = st("Ptile", [128, 8])
    m1c = st("m1c", [128, 1])
    m2c = st("m2c", [128, 1])
    e1t = st("e1t", [128, 8])
    sgt = st("sgt", [128, 8])
    lmask = st("lmask", [128, 8], U32)
    wrow = st("wrow", [8, 64])
    smT = st("smT", [64, 8])
    convst = slabA[:, 11264:11808]
    tailq = slabA[:, 11808:11856].rearrange("p (a c) -> p a c", a=16, c=3)
    segbuf = work[:, 1536:1536 + 2 * WSEG].rearrange(
        "p (a w) -> p a w", a=2, w=WSEG)
    wpbuf = work[0:8, 1664:1920].rearrange("q (a c) -> q a c", a=2, c=128)
    ptTB = work[0:8, 2048:2560]

    from contextlib import ExitStack
    with ExitStack() as _stk:
        p.sems = {}
        for _sn in p.sem_names:
            p.sems[_sn] = _stk.enter_context(nc.semaphore("s" + _sn))
        block = _stk.enter_context(nc.Block())

        psum = [nc.alloc_psum_tensor(f"ps{i}", [128, 512], F32).ap()
                for i in range(8)]
        guards = [[] for _ in range(8)]
        rr = [0]
        held = set()

        def pacq(hold=False):
            while True:
                i = rr[0]
                rr[0] = (rr[0] + 1) % 8
                if i not in held:
                    break
            for tok in guards[i]:
                p.wait("tensor", tok)
            guards[i] = []
            if hold:
                held.add(i)
            return i

        def prel(i):
            held.discard(i)

        def pap(i, rows=128, cols=512, col0=0):
            return psum[i][0:rows, col0:col0 + cols]

        def papb(i, rows=128, cols=128):
            return psum[i].bitcast(BF16)[0:rows, 0:cols]

        def pdone(i, token):
            guards[i].append(token)

        def mm(bi, lhsT, rhs, start, stop, rows=128, cols=512):
            out = pap(bi, rows, cols)
            return p.op("tensor", lambda t, o=out, l=lhsT, r=rhs, s1=start,
                        s2=stop: t.matmul(o, l, r, start=s1, stop=s2))

        def tr(bi, in_, kdim, mrows=128, bf=False):
            out = papb(bi, mrows, kdim) if bf else pap(bi, mrows, kdim)
            idt = (identb_s if bf else identf_s)[0:kdim, 0:kdim]
            return p.op("tensor", lambda t, o=out, i=in_, idd=idt:
                        t.transpose(o, i, idd))

        def vop(fn):
            return p.op("vector", fn)

        def sop(fn):
            return p.op("scalar", fn)

        def gop(fn):
            return p.op("gpsimd", fn)

        def cc(i, n=128):
            return (crow_s if n == 128 else crow16_s)[0:n, i:i + 1]

        xb = [xw0, xw1]

        for dst, src in [(identf_s, identf_d), (identb_s, identb_d),
                         (esq_s, esq_d), (mUneg, mUneg_d), (mLneg, mLneg_d),
                         (mUinc, mUinc_d), (crow_s, crow_d),
                         (crow16_s, crow16_d), (ones_s, onescol_d),
                         (allones_s, allones_d), (id64_s, id64_d),
                         (negbig_s, negbig_d), (gidx_s, gidx_d),
                         (conv_s, conv_d), (dtal_s, dtal_d)]:
            p.dma("sync", dst, src)
        p.dma("sync", winsb, W_in_d.rearrange("(k p) e -> p k e", p=128))
        p.dma("sync", ones16, onesrow_d.broadcast_to([16, 128]))
        p.wait_cur("vector", "qS")
        p.wait_cur("scalar", "qS")
        p.wait_cur("tensor", "qS")
        p.wait_cur("gpsimd", "qS")

        vop(lambda v: v.memset(stc2, 0.0))
        b = pacq()
        for k in range(16):
            mm(b, ones_s, winsb[:, k, :], k == 0, k == 15, rows=1, cols=384)
        p.wait_cur("vector", "eT")
        tv = vop(lambda v, b=b: v.tensor_copy(wincs, pap(b, 1, 384)))
        pdone(b, tv)

        tk = p.dma("sync", xw0[0:8, :], gate_w_d)
        p.wait("tensor", tk)
        for k in range(16):
            b = pacq()
            t = tr(b, xw0[0:8, k * 128:(k + 1) * 128], 8)
            p.wait("vector", t)
            tv = vop(lambda v, k=k, b=b: v.tensor_copy(gwT_s[:, k, :],
                                                       pap(b, 128, 8)))
            pdone(b, tv)

        p.wait_cur("sync", "eT", "eV")
        tk1 = p.dma("sync", xw0[0:16, :], Wb_d)
        tk2 = p.dma("sync", xw0[16:32, :], Wa_d)
        p.wait("tensor", tk1)
        p.wait("tensor", tk2)
        for k in range(16):
            b = pacq()
            t = tr(b, xw0[0:32, k * 128:(k + 1) * 128], 32)
            p.wait("vector", t)
            tv = vop(lambda v, k=k, b=b: v.tensor_copy(wabT_s[:, k, :],
                                                       pap(b, 128, 32)))
            pdone(b, tv)

        p.wait_cur("sync", "eT", "eV")
        xg1 = [None, None]
        xg1s = [None, None]
        for tt in range(NT):
            bi = tt % 2
            p.wait("sync", xg1[bi])
            p.wait("sync", xg1s[bi])
            tk = p.dma("sync", xb[bi], x_in[tt * 128:(tt + 1) * 128, :])
            p.wait("scalar", tk)
            p.wait("vector", tk)
            p.wait_cur("scalar", "eV")
            ts0 = sop(lambda s, bi=bi: s.activation(xn_t, xb[bi], AF.Square))
            vop(lambda v, bi=bi, tt=tt: v.tensor_reduce(
                stats[:, SM_MU0 + tt:SM_MU0 + tt + 1], xb[bi], AXX, ALU.add))
            p.wait("vector", ts0)
            ts = vop(lambda v, tt=tt: v.tensor_reduce(
                stats[:, SM_RS0 + tt:SM_RS0 + tt + 1], xn_t, AXX, ALU.add))
            xg1[bi] = ts
            xg1s[bi] = ts0

        def ln_fin(sum_sl, ss_sl, mu_sl, rstd_sl, nmr_sl, n_, epsc, tmp):
            p.fence("vector")
            vop(lambda v: v.tensor_scalar(mu_sl, sum_sl, 1.0 / n_, None,
                                          ALU.mult))
            vop(lambda v: v.tensor_scalar(ss_sl, ss_sl, 1.0 / n_, None,
                                          ALU.mult))
            vop(lambda v: v.tensor_tensor(tmp, mu_sl, mu_sl, ALU.mult))
            tvx = vop(lambda v: v.tensor_tensor(ss_sl, ss_sl, tmp,
                                                ALU.subtract))
            p.wait("scalar", tvx)
            sop(lambda s: s.activation(ss_sl, ss_sl, AF.Ln, bias=epsc))
            tsx = sop(lambda s: s.activation(rstd_sl, ss_sl, AF.Exp,
                                             scale=-0.5))
            p.wait("vector", tsx)
            vop(lambda v: v.tensor_scalar(nmr_sl, mu_sl, -1.0, None,
                                          ALU.mult))

        ln_fin(stats[:, SM_MU0:SM_MU0 + 16], stats[:, SM_RS0:SM_RS0 + 16],
               stats[:, SM_MU0:SM_MU0 + 16], stats[:, SM_RS0:SM_RS0 + 16],
               stats[:, SM_NM0:SM_NM0 + 16], float(D), cc(CR_E5),
               statc[:, 0:16])

        p.wait_cur("sync", "eS")
        tke = p.dma("sync", wrow, eaw_d)
        p.wait("vector", tke)
        vop(lambda v: v.tensor_reduce(sscol[0:8, 0:1], wrow, AXX, ALU.max))
        p.fence("vector")
        tv = vop(lambda v: v.tensor_scalar(rsqc[0:8, 0:1], sscol[0:8, 0:1],
                                           -1.0, None, ALU.mult))
        p.wait("scalar", tv)
        te = sop(lambda s: s.activation(wrow, wrow, AF.Exp,
                                        bias=rsqc[0:8, 0:1]))
        p.wait("vector", te)
        tv = vop(lambda v: v.tensor_reduce(sscol[0:8, 0:1], wrow, AXX,
                                           ALU.add))
        p.wait("scalar", tv)
        sop(lambda s: s.activation(m1c[0:8, 0:1], sscol[0:8, 0:1], AF.Ln))
        ts = sop(lambda s: s.activation(m1c[0:8, 0:1], m1c[0:8, 0:1], AF.Exp,
                                        scale=-1.0))
        p.wait("vector", ts)
        tv = vop(lambda v: v.tensor_scalar(wrow, wrow, m1c[0:8, 0:1], None,
                                           ALU.mult))
        b = pacq()
        p.wait("tensor", tv)
        t = tr(b, wrow, 8, mrows=64)
        p.wait("vector", t)
        tv = vop(lambda v, b=b: v.tensor_copy(smT, pap(b, 64, 8)))
        pdone(b, tv)
        for ch in range(32):
            p.wait_cur("sync", "eV")
            tka = p.dma("sync", xw0[0:64, 0:512],
                        atoms_d[:, ch * 512:(ch + 1) * 512])
            tki = p.dma("sync", xw1[0:8, 0:512],
                        imp_d[:, ch * 512:(ch + 1) * 512])
            b = pacq()
            p.wait("tensor", tka)
            t = mm(b, smT, xw0[0:64, 0:512], True, True, rows=8)
            p.wait("scalar", tki)
            ts = sop(lambda s: s.activation(xw1[0:8, 0:512], xw1[0:8, 0:512],
                                            AF.Sigmoid))
            p.wait("vector", t)
            p.wait("vector", ts)
            tv = vop(lambda v, b=b: v.tensor_tensor(
                xw0[0:8, 0:512], pap(b, 8, 512), xw1[0:8, 0:512], ALU.mult))
            pdone(b, tv)
            p.wait("sync", tv)
            p.dma("sync", wp_scr[:, ch * 512:(ch + 1) * 512], xw0[0:8, 0:512])

        p.barrier()

        if upto >= 1:
            for tt in range(NT):
                bi = tt % 2
                p.wait_cur("scalar", "eT", "eV")
                p.wait_cur("vector", "eT", "eS")
                p.wait("sync", xg1[bi])
                p.wait_cur("sync", "eT")
                tkx = p.dma("sync", xb[bi], x_in[tt * 128:(tt + 1) * 128, :])
                p.wait("vector", tkx)
                t1 = vop(lambda v, bi=bi, tt=tt: v.tensor_scalar(
                    xn_t, xb[bi],
                    stats[:, SM_NM0 + tt:SM_NM0 + tt + 1],
                    stats[:, SM_RS0 + tt:SM_RS0 + tt + 1],
                    ALU.add, ALU.mult))
                xg1[bi] = t1
                p.wait("scalar", t1)
                t2 = sop(lambda s, bi=bi: s.activation(wsq_t, xn_t,
                                                       AF.Square))
                vop(lambda v: v.tensor_reduce(statc[:, 20:21], xn_t, AXX,
                                              ALU.add))
                tamax = vop(lambda v: v.reduce_max(
                    statc[:, 22:23], xn_t, AXX, apply_absolute_value=True))
                p.wait("vector", t2)
                vop(lambda v: v.tensor_reduce(statc[:, 24:25], wsq_t, AXX,
                                              ALU.add))
                p.fence("vector")
                vop(lambda v: v.tensor_scalar(statc[:, 21:22], statc[:, 20:21],
                                              1.0 / D, None, ALU.mult))
                vop(lambda v: v.tensor_scalar(statc[:, 24:25], statc[:, 24:25],
                                              1.0 / D, None, ALU.mult))
                vop(lambda v: v.tensor_tensor(statc[:, 26:27], statc[:, 21:22],
                                              statc[:, 21:22], ALU.mult))
                tvv = vop(lambda v: v.tensor_tensor(
                    statc[:, 27:28], statc[:, 24:25], statc[:, 26:27],
                    ALU.subtract))
                p.wait("scalar", tvv)
                sop(lambda s: s.activation(statc[:, 28:29], statc[:, 27:28],
                                           AF.Ln, bias=cc(CR_E5)))
                ts1 = sop(lambda s, tt=tt: s.activation(
                    rstd2_s[:, tt:tt + 1], statc[:, 28:29], AF.Exp,
                    scale=-0.5))
                sop(lambda s: s.activation(statc[:, 29:30], statc[:, 24:25],
                                           AF.Ln, bias=cc(CR_E5)))
                ts2 = sop(lambda s: s.activation(
                    statc[:, 30:31], statc[:, 29:30], AF.Exp, scale=-0.5))
                p.wait("vector", ts1)
                p.wait("vector", ts2)
                vop(lambda v, tt=tt: v.tensor_tensor(
                    statc[:, 31:32], statc[:, 21:22], rstd2_s[:, tt:tt + 1],
                    ALU.mult))
                vop(lambda v, tt=tt: v.tensor_scalar(
                    nmr2_s[:, tt:tt + 1], statc[:, 31:32], -1.0, None,
                    ALU.mult))
                p.wait("vector", tamax)
                vop(lambda v, tt=tt: v.tensor_copy(s1sum[:, tt:tt + 1],
                                                   statc[:, 22:23]))
                vop(lambda v, tt=tt: v.tensor_copy(s1ss[:, tt:tt + 1],
                                                   statc[:, 24:25]))
                vop(lambda v: v.tensor_tensor(statc[:, 32:33], statc[:, 22:23],
                                              statc[:, 30:31], ALU.mult))
                tvv = vop(lambda v: v.tensor_scalar(
                    statc[:, 32:33], statc[:, 32:33], 1e-5, None, ALU.max))
                p.wait("scalar", tvv)
                sop(lambda s: s.activation(statc[:, 33:34], statc[:, 32:33],
                                           AF.Ln))
                ts3 = sop(lambda s: s.activation(
                    statc[:, 34:35], statc[:, 33:34], AF.Exp, scale=-1.0,
                    bias=cc(CR_L127)))
                p.wait("vector", ts3)
                vop(lambda v, tt=tt: v.tensor_tensor(
                    qsc_s[:, tt:tt + 1], statc[:, 30:31], statc[:, 34:35],
                    ALU.mult))
                vop(lambda v, tt=tt: v.tensor_scalar(
                    invq_s[:, tt:tt + 1], statc[:, 32:33], 1.0 / 127.0, None,
                    ALU.mult))
                tvv = vop(lambda v: v.tensor_scalar(
                    stc2[:, 0:1], statc[:, 21:22], -1.0, None, ALU.mult))
                b = pacq()
                p.wait("tensor", tvv)
                t = tr(b, stc2, 128, mrows=2)
                p.wait("vector", t)
                tvv = vop(lambda v, b=b: v.tensor_copy(strow, pap(b, 2, 128)))
                pdone(b, tvv)

                bgate = pacq(hold=True)
                prev_mm = None
                for k in range(16):
                    xbi = k % 2
                    btr = pacq()
                    ttk = tr(btr, xn_t[:, k * 128:(k + 1) * 128], 128)
                    p.wait("vector", ttk)
                    p.wait("vector", prev_mm)
                    tvv = vop(lambda v, b=btr, xbi=xbi: v.tensor_copy(
                        xnT2[:, xbi, :], pap(b, 128, 128)))
                    pdone(btr, tvv)
                    p.wait("tensor", tvv)
                    prev_mm = mm(bgate, xnT2[:, xbi, :], winsb[:, k, :],
                                 k == 0, False, cols=384)
                mm(bgate, strow[0:1, :], wincs, False, True, cols=384)
                p.wait_cur("vector", "eT")
                tvv = vop(lambda v, b=bgate, tt=tt: v.tensor_scalar(
                    gat, pap(b, 128, 384), rstd2_s[:, tt:tt + 1], None,
                    ALU.mult))
                pdone(bgate, tvv)
                prel(bgate)
                p.wait("scalar", tvv)
                ta1 = sop(lambda s: s.activation(wA[:, 0:128], gat[:, 0:128],
                                                 AF.Sigmoid))
                ta2 = sop(lambda s: s.activation(wB[:, 0:128], gat[:, 0:128],
                                                 AF.Sigmoid, scale=-1.0))
                ta3 = sop(lambda s: s.activation(wC[:, 0:128],
                                                 gat[:, 256:384], AF.Tanh))
                p.wait("vector", ta1)
                tvv = vop(lambda v: v.tensor_tensor(
                    wD[:, 0:128], wA[:, 0:128], gat[:, 128:256], ALU.mult))
                p.wait("tensor", ta2)
                p.wait("tensor", ta3)
                p.wait("tensor", tvv)
                for src, dstarr in ((wB[:, 0:128], scan_oma),
                                    (wD[:, 0:128], scan_ab),
                                    (wC[:, 0:128], scan_tc)):
                    b2 = pacq()
                    ttk = tr(b2, src, 128)
                    p.wait("vector", ttk)
                    tvv = vop(lambda v, b=b2, dst=dstarr, tt=tt:
                              v.tensor_copy(dst[:, tt * 128:(tt + 1) * 128],
                                            pap(b, 128, 128)))
                    pdone(b2, tvv)

                p.wait_cur("scalar", "eV", "eT")
                p.wait_cur("vector", "eT")
                for hf in range(2):
                    sl = slice(hf * 1024, (hf + 1) * 1024)
                    tq = vop(lambda v, sl=sl, tt=tt: v.tensor_scalar(
                        wA, xn_t[:, sl], qsc_s[:, tt:tt + 1], None,
                        ALU.mult))
                    vop(lambda v: v.tensor_scalar(wB, wA, C32, -C32, ALU.add,
                                                  ALU.add))
                    vop(lambda v: v.tensor_scalar(wC, wB, C52, -C52, ALU.add,
                                                  ALU.add))
                    vop(lambda v: v.tensor_tensor(wC, wC, wB, ALU.subtract))
                    tq = vop(lambda v: v.tensor_tensor(wD, wC, wC, ALU.mult))
                    p.wait("scalar", tq)
                    tq = sop(lambda s: s.activation(wD, wD, AF.Sign,
                                                    bias=cc(CR_NQ)))
                    p.wait("vector", tq)
                    vop(lambda v: v.scalar_tensor_tensor(wA, wD, 0.0, wC,
                                                         ALU.max, ALU.mult))
                    vop(lambda v: v.tensor_tensor(wB, wB, wC, ALU.add))
                    vop(lambda v: v.tensor_tensor(wB, wB, wA, ALU.subtract))
                    vop(lambda v: v.tensor_scalar(wB, wB, -128.0, 127.0,
                                                  ALU.max, ALU.min))
                    tq = vop(lambda v: v.tensor_copy(qbf[:, 0:1024], wB))
                    p.wait("scalar", tq)
                    p.wait_cur("tensor", "eV")
                    for j in range(8):
                        k = hf * 8 + j
                        b2 = pacq()
                        ttk = tr(b2, qbf[:, j * 128:(j + 1) * 128], 128,
                                 bf=True)
                        p.wait("vector", ttk)
                        tvv = vop(lambda v, b=b2, k=k, tt=tt: v.tensor_copy(
                            xqT[:, k, tt * 128:(tt + 1) * 128],
                            papb(b, 128, 128)))
                        pdone(b2, tvv)
            p.barrier()

        if upto >= 2:
            vop(lambda v: v.tensor_tensor_scan(
                yfm, scan_oma, scan_ab, 0.0, ALU.mult, ALU.add))
            p.fence("vector")
            vop(lambda v: v.tensor_tensor(yfm, yfm, scan_tc, ALU.mult))
            p.barrier()

        if upto >= 3:
            def wmean(w_d, nrow, ncol, col_out):
                ntile = (nrow // 128) * (ncol // 2048)
                ncb = ncol // 2048
                wg = [None, None]
                for i in range(ntile):
                    bi = i % 2
                    rb, cb = i // ncb, i % ncb
                    p.wait("sync", wg[bi])
                    tk = p.dma("sync", xb[bi],
                               w_d[rb * 128:(rb + 1) * 128,
                                   cb * 2048:(cb + 1) * 2048])
                    p.wait("scalar", tk)
                    p.wait_cur("scalar", "eV")
                    tsa = sop(lambda s, bi=bi: s.activation(
                        xn_t, xb[bi], AF.Abs))
                    p.wait("vector", tsa)
                    ts = vop(lambda v, i=i: v.tensor_reduce(
                        acc64[:, i % 64:i % 64 + 1], xn_t, AXX, ALU.add))
                    wg[bi] = tsa
                nred = min(ntile, 64)
                p.fence("vector")
                tv = vop(lambda v, n=nred: v.tensor_reduce(
                    statc[:, 36:37], acc64[:, 0:n], AXX, ALU.add))
                b = pacq()
                p.wait("tensor", tv)
                mm(b, allones_s, statc[:, 36:37], True, True, rows=128,
                   cols=1)
                p.wait_cur("vector", "eT")
                tv = vop(lambda v, b=b, nn=nrow * ncol, c=col_out:
                         v.tensor_scalar(swcol[:, c:c + 1], pap(b, 128, 1),
                                         1.0 / nn, None, ALU.mult))
                pdone(b, tv)
                tv = vop(lambda v, c=col_out: v.tensor_scalar(
                    swcol[:, c:c + 1], swcol[:, c:c + 1], 1e-5, None,
                    ALU.max))
                p.wait("scalar", tv)
                sop(lambda s, c=col_out: s.activation(
                    statc[:, 38:39], swcol[:, c:c + 1], AF.Ln))
                ts = sop(lambda s, c=col_out: s.activation(
                    swcol[:, c + 1:c + 2], statc[:, 38:39], AF.Exp,
                    scale=-1.0))
                p.wait("vector", ts)

            wmean(w1_d, DI, D, 0)
            p.barrier()
            wmean(w2_d, D, DI, 2)
            p.barrier()
            vop(lambda v: v.tensor_scalar(comb1_s, invq_s, swcol[:, 0:1],
                                          None, ALU.mult))
            p.barrier()

        if upto >= 4:
            hcons = [None, None, None, None]
            hdma = [None, None, None, None]
            wg = [None, None]
            for hc in range(16):
                for k in range(16):
                    bi = k % 2
                    p.wait("sync", wg[bi])
                    p.wait_cur("sync", "eS")
                    src = w1_d[hc * 512:(hc + 1) * 512,
                               k * 128:(k + 1) * 128].rearrange(
                                   "(a p) c -> p a c", p=128)
                    tk = p.dma("sync", xb[bi][:, 0:512], src)
                    p.wait("scalar", tk)
                    p.wait_cur("scalar", "eT", "eV")
                    sop(lambda s, bi=bi: s.activation(
                        xn_t[:, 0:512], xb[bi][:, 0:512], AF.Abs,
                        scale=swcol[:, 1:2]))
                    sop(lambda s, bi=bi: s.activation(
                        xn_t[:, 512:1024], xb[bi][:, 0:512], AF.Sign))
                    sm_ = sop(lambda s: s.activation(
                        xn_t[:, 1024:1536], xn_t[:, 0:512], AF.Sign,
                        scale=-1.0, bias=cc(CR_HALF)))
                    wg[bi] = sm_
                    p.wait("vector", sm_)
                    vop(lambda v: v.scalar_tensor_tensor(
                        xn_t[:, 1536:2048], xn_t[:, 1024:1536], 0.0,
                        xn_t[:, 512:1024], ALU.max, ALU.mult))
                    tv = vop(lambda v: v.tensor_tensor(
                        xn_t[:, 0:512], xn_t[:, 512:1024],
                        xn_t[:, 1536:2048], ALU.subtract))
                    p.wait("tensor", tv)
                    for bb in range(4):
                        b2 = pacq()
                        ttk = tr(b2, xn_t[:, bb * 128:bb * 128 + 128], 128)
                        p.wait("vector", ttk)
                        tvv = vop(lambda v, b=b2, k=k, bb=bb: v.tensor_copy(
                            wq1T[:, k, bb * 128:(bb + 1) * 128],
                            pap(b, 128, 128)))
                        pdone(b2, tvv)
                p.wait_cur("tensor", "eV")
                for tt in range(NT):
                    b = pacq()
                    for k in range(16):
                        mm(b, xqT[:, k, tt * 128:(tt + 1) * 128],
                           wq1T[:, k, :], k == 0, k == 15)
                    hb = tt % 4
                    p.wait("scalar", hdma[hb])
                    p.wait("scalar", hcons[hb])
                    p.wait_cur("scalar", "eT", "eV")
                    s1 = sop(lambda s, b=b, tt=tt, hb=hb: s.activation(
                        hact4[:, hb, :], pap(b), AF.Silu,
                        scale=comb1_s[:, tt:tt + 1]))
                    pdone(b, s1)
                    s2 = sop(lambda s, hb=hb: s.activation(
                        hact4[:, hb, :], hact4[:, hb, :], AF.Gelu))
                    p.wait("vector", s2)
                    p.wait("gpsimd", s2)
                    ci = tt * 16 + hc
                    vop(lambda v, hb=hb, ci=ci: v.tensor_reduce(
                        hstat[:, 0, ci:ci + 1], hact4[:, hb, :], AXX, ALU.add))
                    sacc = sop(lambda s, hb=hb: s.activation(
                        wA[:, 0:512], hact4[:, hb, :], AF.Square))
                    p.wait("vector", sacc)
                    vop(lambda v, ci=ci: v.tensor_reduce(
                        hstat[:, 1, ci:ci + 1], wA[:, 0:512], AXX, ALU.add))
                    vop(lambda v, hb=hb, ci=ci: v.tensor_reduce(
                        hstat[:, 2, ci:ci + 1], hact4[:, hb, :], AXX, ALU.max))
                    g3 = vop(lambda v, hb=hb, ci=ci: v.tensor_reduce(
                        hstat[:, 3, ci:ci + 1], hact4[:, hb, :], AXX, ALU.min))
                    hcons[hb] = g3
                    p.wait("scalar", g3)
                    tkh = p.dma("scalar",
                                h_scr[tt, :, hc * 512:(hc + 1) * 512],
                                hact4[:, hb, :])
                    hdma[hb] = tkh
            p.barrier()

            w2g = [None, None]
            stg = [None, None]
            for dc in range(16):
                for kk in range(16):
                    bi = kk % 2
                    p.wait("sync", w2g[bi])
                    tk = p.dma("sync", xb[bi][:, 0:512],
                               w2_d[dc * 128:(dc + 1) * 128,
                                    kk * 512:(kk + 1) * 512])
                    p.wait("scalar", tk)
                    p.wait_cur("scalar", "eT", "eV")
                    sop(lambda s, bi=bi: s.activation(
                        xn_t[:, 0:512], xb[bi][:, 0:512], AF.Abs,
                        scale=swcol[:, 3:4]))
                    sop(lambda s, bi=bi: s.activation(
                        xn_t[:, 512:1024], xb[bi][:, 0:512], AF.Sign))
                    sm_ = sop(lambda s: s.activation(
                        xn_t[:, 1024:1536], xn_t[:, 0:512], AF.Sign,
                        scale=-1.0, bias=cc(CR_HALF)))
                    w2g[bi] = sm_
                    p.wait("vector", sm_)
                    vop(lambda v: v.scalar_tensor_tensor(
                        xn_t[:, 1536:2048], xn_t[:, 1024:1536], 0.0,
                        xn_t[:, 512:1024], ALU.max, ALU.mult))
                    tv = vop(lambda v: v.tensor_tensor(
                        xn_t[:, 0:512], xn_t[:, 512:1024],
                        xn_t[:, 1536:2048], ALU.subtract))
                    p.wait("tensor", tv)
                    for bb in range(4):
                        b2 = pacq()
                        ttk = tr(b2, xn_t[:, bb * 128:bb * 128 + 128], 128)
                        sbi = bb % 2
                        p.wait("vector", stg[sbi])
                        p.wait("vector", ttk)
                        tvv = vop(lambda v, b=b2, sbi=sbi: v.tensor_copy(
                            w2stage[:, sbi * 128:(sbi + 1) * 128],
                            pap(b, 128, 128)))
                        pdone(b2, tvv)
                        p.wait("gpsimd", tvv)
                        tkw = p.dma(
                            "gpsimd",
                            wq2T_scr[kk * 512 + bb * 128:
                                     kk * 512 + (bb + 1) * 128,
                                     dc * 128:(dc + 1) * 128],
                            w2stage[:, sbi * 128:(sbi + 1) * 128])
                        stg[sbi] = tkw
            p.barrier()

        if upto >= 5:
            for tt in range(NT):
                sl = slice(tt * 16, (tt + 1) * 16)
                vop(lambda v, sl=sl: v.tensor_reduce(
                    statc[:, 0:1], hstat[:, 0, sl], AXX, ALU.add))
                vop(lambda v, sl=sl: v.tensor_reduce(
                    statc[:, 1:2], hstat[:, 1, sl], AXX, ALU.add))
                vop(lambda v, sl=sl: v.tensor_reduce(
                    statc[:, 2:3], hstat[:, 2, sl], AXX, ALU.max))
                vop(lambda v, sl=sl: v.tensor_reduce(
                    statc[:, 3:4], hstat[:, 3, sl], AXX, ALU.min))
                p.fence("vector")
                vop(lambda v: v.tensor_scalar(statc[:, 4:5], statc[:, 0:1],
                                              1.0 / DI, None, ALU.mult))
                vop(lambda v: v.tensor_scalar(statc[:, 5:6], statc[:, 1:2],
                                              1.0 / DI, None, ALU.mult))
                vop(lambda v: v.tensor_tensor(statc[:, 6:7], statc[:, 4:5],
                                              statc[:, 4:5], ALU.mult))
                tv = vop(lambda v: v.tensor_tensor(
                    statc[:, 7:8], statc[:, 5:6], statc[:, 6:7], ALU.subtract))
                p.wait("scalar", tv)
                sop(lambda s: s.activation(statc[:, 8:9], statc[:, 7:8],
                                           AF.Ln, bias=cc(CR_E5)))
                ts1 = sop(lambda s: s.activation(statc[:, 9:10],
                                                 statc[:, 8:9], AF.Exp,
                                                 scale=-0.5))
                ts2 = sop(lambda s: s.activation(statc[:, 10:11],
                                                 statc[:, 8:9], AF.Exp,
                                                 scale=-1.0))
                p.wait("vector", ts2)
                tv = vop(lambda v: v.tensor_tensor(
                    statc[:, 11:12], statc[:, 7:8], statc[:, 10:11], ALU.mult))
                p.wait("scalar", tv)
                sop(lambda s: s.activation(statc[:, 12:13], statc[:, 11:12],
                                           AF.Ln, bias=cc(CR_E5)))
                ts1b = sop(lambda s: s.activation(
                    statc[:, 13:14], statc[:, 12:13], AF.Exp, scale=-0.5))
                p.wait("vector", ts1)
                p.wait("vector", ts1b)
                vop(lambda v: v.tensor_tensor(statc[:, 14:15], statc[:, 2:3],
                                              statc[:, 4:5], ALU.subtract))
                vop(lambda v: v.tensor_tensor(statc[:, 15:16], statc[:, 4:5],
                                              statc[:, 3:4], ALU.subtract))
                vop(lambda v: v.tensor_tensor(statc[:, 14:15], statc[:, 14:15],
                                              statc[:, 15:16], ALU.max))
                vop(lambda v: v.tensor_tensor(statc[:, 14:15], statc[:, 14:15],
                                              statc[:, 9:10], ALU.mult))
                vop(lambda v: v.tensor_tensor(statc[:, 14:15], statc[:, 14:15],
                                              statc[:, 13:14], ALU.mult))
                tv = vop(lambda v: v.tensor_scalar(
                    statc[:, 14:15], statc[:, 14:15], 1e-5, None, ALU.max))
                p.wait("scalar", tv)
                sop(lambda s: s.activation(statc[:, 16:17], statc[:, 14:15],
                                           AF.Ln))
                ts3 = sop(lambda s: s.activation(
                    statc[:, 17:18], statc[:, 16:17], AF.Exp, scale=-1.0,
                    bias=cc(CR_L127)))
                p.wait("vector", ts3)
                vop(lambda v: v.tensor_tensor(statc[:, 18:19], statc[:, 9:10],
                                              statc[:, 13:14], ALU.mult))
                vop(lambda v, tt=tt: v.tensor_tensor(
                    hsc_s[:, tt:tt + 1], statc[:, 18:19], statc[:, 17:18],
                    ALU.mult))
                vop(lambda v, tt=tt: v.tensor_scalar(
                    hbc_s[:, tt:tt + 1], statc[:, 4:5], -1.0, None,
                    ALU.mult))
                vop(lambda v: v.tensor_scalar(
                    statc[:, 19:20], statc[:, 14:15], 1.0 / 127.0, None,
                    ALU.mult))
                vop(lambda v, tt=tt: v.tensor_scalar(
                    comb2_s[:, tt:tt + 1], statc[:, 19:20], swcol[:, 2:3],
                    None, ALU.mult))
            p.barrier()
            p.dma("sync", wout0_s, W_out0_d)

            for ST in range(4):
                for t4 in range(4):
                    tt = ST * 4 + t4
                    for e8 in range(8):
                        p.wait_cur("sync", "eS", "eV", "eT")
                        tkh = p.dma("sync", xn_t[:, 0:1024],
                                    h_scr[tt, :, e8 * 1024:(e8 + 1) * 1024])
                        p.wait("vector", tkh)
                        tq = vop(lambda v, tt=tt: v.tensor_scalar(
                            wA, xn_t[:, 0:1024],
                            hbc_s[:, tt:tt + 1], hsc_s[:, tt:tt + 1],
                            ALU.add, ALU.mult))
                        vop(lambda v: v.tensor_scalar(wB, wA, C32, -C32,
                                                      ALU.add, ALU.add))
                        vop(lambda v: v.tensor_scalar(wC, wB, C52, -C52,
                                                      ALU.add, ALU.add))
                        vop(lambda v: v.tensor_tensor(wC, wC, wB,
                                                      ALU.subtract))
                        tq = vop(lambda v: v.tensor_tensor(wD, wC, wC,
                                                           ALU.mult))
                        p.wait("scalar", tq)
                        tq = sop(lambda s: s.activation(wD, wD, AF.Sign,
                                                        bias=cc(CR_NQ)))
                        p.wait("vector", tq)
                        vop(lambda v: v.scalar_tensor_tensor(
                            wA, wD, 0.0, wC, ALU.max, ALU.mult))
                        vop(lambda v: v.tensor_tensor(wB, wB, wC, ALU.add))
                        vop(lambda v: v.tensor_tensor(wB, wB, wA,
                                                      ALU.subtract))
                        vop(lambda v: v.tensor_scalar(wB, wB, -128.0, 127.0,
                                                      ALU.max, ALU.min))
                        tq = vop(lambda v: v.tensor_copy(
                            qbf[:, 0:1024], wB))
                        p.wait_cur("tensor", "eV")
                        for j in range(8):
                            kk = e8 * 8 + j
                            b2 = pacq()
                            ttk = tr(b2, qbf[:, j * 128:(j + 1) * 128],
                                     128, bf=True)
                            p.wait("vector", ttk)
                            tvv = vop(lambda v, b=b2, kk=kk, t4=t4:
                                      v.tensor_copy(
                                          xq2T[:, kk,
                                               t4 * 128:(t4 + 1) * 128],
                                          papb(b, 128, 128)))
                            pdone(b2, tvv)
                w2tg = [None, None]
                for dc in range(4):
                    bts = [pacq(), pacq(), pacq(), pacq()]
                    for kk in range(64):
                        bi = kk % 2
                        p.wait("sync", w2tg[bi])
                        tkw = p.dma("sync", w2t2[:, bi, :],
                                    wq2T_scr[kk * 128:(kk + 1) * 128,
                                             dc * 512:(dc + 1) * 512])
                        p.wait("tensor", tkw)
                        lastmm = None
                        for t4 in range(4):
                            lastmm = mm(bts[t4],
                                        xq2T[:, kk, t4 * 128:(t4 + 1) * 128],
                                        w2t2[:, bi, :], kk == 0, kk == 63)
                        w2tg[bi] = lastmm
                    p.wait_cur("scalar", "eT")
                    for t4 in range(4):
                        tt = ST * 4 + t4
                        sg1 = sop(lambda s, b=bts[t4], tt=tt: s.activation(
                            wA[:, 0:512], pap(b), AF.Silu,
                            scale=comb2_s[:, tt:tt + 1]))
                        pdone(bts[t4], sg1)
                        p.wait("sync", sg1)
                        tkg = p.dma("sync",
                                    g_scr[tt * 128:(tt + 1) * 128,
                                          dc * 512:(dc + 1) * 512],
                                    wA[:, 0:512])
                        p.wait("scalar", tkg)
                for t4 in range(4):
                    tt = ST * 4 + t4
                    bi = t4 % 2
                    p.wait("sync", xg1[bi])
                    p.wait_cur("sync", "eT")
                    tkx = p.dma("sync", xb[bi],
                                x_in[tt * 128:(tt + 1) * 128, :])
                    p.wait("vector", tkx)
                    p.wait_cur("vector", "eT", "eS")
                    ts1 = vop(lambda v, bi=bi, tt=tt: v.tensor_scalar(
                        xn_t, xb[bi],
                        stats[:, SM_NM0 + tt:SM_NM0 + tt + 1],
                        stats[:, SM_RS0 + tt:SM_RS0 + tt + 1],
                        ALU.add, ALU.mult))
                    xg1[bi] = ts1
                    p.wait("tensor", ts1)
                    for k in range(16):
                        b2 = pacq()
                        ttk = tr(b2, xn_t[:, k * 128:(k + 1) * 128], 128)
                        p.wait("vector", ttk)
                        tvv = vop(lambda v, b=b2, k=k, t4=t4: v.tensor_copy(
                            xnTst[:, k, t4 * 128:(t4 + 1) * 128],
                            pap(b, 128, 128)))
                        pdone(b2, tvv)
                p.wait_cur("sync", "eV", "eT")
                wsg = [None, None]
                for dc in range(4):
                    bts = [pacq(), pacq(), pacq(), pacq()]
                    p.wait_cur("tensor", "eV")
                    for t4 in range(4):
                        tt = ST * 4 + t4
                        mm(bts[t4], yfm[:, tt * 128:(tt + 1) * 128],
                           wout0_s[:, dc * 512:(dc + 1) * 512], True, False)
                    for k in range(16):
                        bi = k % 2
                        p.wait("sync", wsg[bi])
                        tkw = p.dma("sync", xb[bi][:, 0:512],
                                    W_skip_d[k * 128:(k + 1) * 128,
                                             dc * 512:(dc + 1) * 512])
                        p.wait("tensor", tkw)
                        lastmm = None
                        for t4 in range(4):
                            lastmm = mm(bts[t4],
                                        xnTst[:, k, t4 * 128:(t4 + 1) * 128],
                                        xb[bi][:, 0:512], False, k == 15)
                        wsg[bi] = lastmm
                    p.wait_cur("vector", "eT")
                    for t4 in range(4):
                        tt = ST * 4 + t4
                        p.wait_cur("scalar", "eV")
                        tkg = p.dma("scalar", wC[:, 0:512],
                                    g_scr[tt * 128:(tt + 1) * 128,
                                          dc * 512:(dc + 1) * 512])
                        tkx = p.dma("scalar", wD[:, 0:512],
                                    x_in[tt * 128:(tt + 1) * 128,
                                         dc * 512:(dc + 1) * 512])
                        p.wait("vector", tkg)
                        p.wait("vector", tkx)
                        tv1 = vop(lambda v, b=bts[t4]: v.tensor_tensor(
                            wA[:, 0:512], pap(b), wC[:, 0:512], ALU.mult))
                        pdone(bts[t4], tv1)
                        tv2 = vop(lambda v: v.tensor_tensor(
                            wB[:, 0:512], wA[:, 0:512], wD[:, 0:512],
                            ALU.add))
                        p.wait("scalar", tv2)
                        vop(lambda v, dc=dc: v.tensor_reduce(
                            statc[:, 20 + dc:21 + dc], wB[:, 0:512], AXX,
                            ALU.add))
                        s1b0 = sop(lambda s: s.activation(
                            wA[:, 0:512], wB[:, 0:512], AF.Square))
                        p.wait("vector", s1b0)
                        s1b = vop(lambda v, dc=dc: v.tensor_reduce(
                            statc[:, 24 + dc:25 + dc], wA[:, 0:512], AXX,
                            ALU.add))
                        p.wait("sync", s1b)
                        tk1 = p.dma("sync",
                                    x1_scr[tt * 128:(tt + 1) * 128,
                                           dc * 512:(dc + 1) * 512],
                                    wB[:, 0:512])
                        p.wait("vector", tk1)
                        p.fence("vector")
                        if dc == 0:
                            vop(lambda v, tt=tt: v.tensor_copy(
                                s1sum[:, tt:tt + 1], statc[:, 20:21]))
                            vop(lambda v, tt=tt: v.tensor_copy(
                                s1ss[:, tt:tt + 1], statc[:, 24:25]))
                        else:
                            vop(lambda v, tt=tt, dc=dc: v.tensor_tensor(
                                s1sum[:, tt:tt + 1], s1sum[:, tt:tt + 1],
                                statc[:, 20 + dc:21 + dc], ALU.add))
                            vop(lambda v, tt=tt, dc=dc: v.tensor_tensor(
                                s1ss[:, tt:tt + 1], s1ss[:, tt:tt + 1],
                                statc[:, 24 + dc:25 + dc], ALU.add))
            p.barrier()
            ln_fin(s1sum[:, 0:NT], s1ss[:, 0:NT], mu1_s[:, 0:NT],
                   rstd1_s[:, 0:NT], nmr1_s[:, 0:NT], float(D), cc(CR_E5),
                   statc[:, 0:16])
            p.barrier()

        if upto >= 6:
            def pretrans(w_d, nrow, ncol, dst_scr):
                p.wait_cur("sync", "eT", "eV")
                p.wait_cur("vector", "qG")
                wg2 = [None, None]
                sg2 = [None, None]
                for rb in range(nrow // 128):
                    bi = rb % 2
                    p.wait("sync", wg2[bi])
                    tk = p.dma("sync", xb[bi][:, 0:ncol],
                               w_d[rb * 128:(rb + 1) * 128, :])
                    p.wait("tensor", tk)
                    last = None
                    for cb in range(ncol // 128):
                        b2 = pacq()
                        ttk = tr(b2, xb[bi][:, cb * 128:(cb + 1) * 128], 128)
                        p.wait("vector", ttk)
                        sbi = cb % 2
                        p.wait("vector", sg2[sbi])
                        tvv = vop(lambda v, b=b2, sbi=sbi: v.tensor_copy(
                            wA[:, 0:128] if sbi == 0 else wB[:, 0:128],
                            pap(b, 128, 128)))
                        pdone(b2, tvv)
                        p.wait("gpsimd", tvv)
                        tkw = p.dma("gpsimd",
                                    dst_scr[cb * 128:(cb + 1) * 128,
                                            rb * 128:(rb + 1) * 128],
                                    wA[:, 0:128] if sbi == 0 else
                                    wB[:, 0:128])
                        sg2[sbi] = tkw
                        last = tvv
                    wg2[bi] = last

            pretrans(W_qkv_d, D, D, wqkvT_scr)
            pretrans(W_z_d, 1024, D, wzT_scr)
            pretrans(Wffn_d, D, D, wT_scr)
            pretrans(W_out1_d, D, 1024, woutT_scr)
            ts = sop(lambda s: s.activation(negea, dtal_s[:, 1:2], AF.Exp))
            p.wait("vector", ts)
            vop(lambda v: v.tensor_scalar(negea, negea, -1.0, None, ALU.mult))
            gop(lambda g: g.memset(tailq, 0.0))
            p.barrier()

            for TB in range(4):
                for t4 in range(4):
                    tt = TB * 4 + t4
                    bi = t4 % 2
                    p.wait("sync", xg1[bi])
                    p.wait_cur("sync", "eT")
                    tkx = p.dma("sync", xb[bi],
                                x1_scr[tt * 128:(tt + 1) * 128, :])
                    p.wait("vector", tkx)
                    p.wait_cur("vector", "eT", "eS")
                    ts1 = vop(lambda v, bi=bi, tt=tt: v.tensor_scalar(
                        xn_t, xb[bi], nmr1_s[:, tt:tt + 1],
                        rstd1_s[:, tt:tt + 1], ALU.add, ALU.mult))
                    xg1[bi] = ts1
                    p.wait("tensor", ts1)
                    for k in range(16):
                        b2 = pacq()
                        ttk = tr(b2, xn_t[:, k * 128:(k + 1) * 128], 128)
                        p.wait("vector", ttk)
                        tvv = vop(lambda v, b=b2, k=k, t4=t4: v.tensor_copy(
                            x1TBbuf[:, k, t4 * 128:(t4 + 1) * 128],
                            pap(b, 128, 128)))
                        pdone(b2, tvv)
                p.wait_cur("gpsimd", "eV")
                p.wait_cur("tensor", "eV")
                for k in range(16):
                    p.dma("gpsimd",
                          xn1T_scr[k * 128:(k + 1) * 128,
                                   TB * 512:(TB + 1) * 512],
                          x1TBbuf[:, k, :])

                qg = [None, None]
                rsqbank = [None]
                for m in range(16):
                    bi = m % 2
                    p.wait("sync", qg[bi])
                    tkw = p.dma("sync", wqkvbuf[:, bi, :, :],
                                wqkvT_scr[:, m * 128:(m + 1) * 128].rearrange(
                                    "(k p) c -> p k c", p=128))
                    p.wait("tensor", tkw)
                    b = pacq()
                    lst = None
                    for k in range(16):
                        lst = mm(b, wqkvbuf[:, bi, k, :], x1TBbuf[:, k, :],
                                 k == 0, k == 15)
                    qg[bi] = lst
                    p.wait_cur("vector", "eT", "eS")
                    vop(lambda v, m=m: v.tensor_copy(convst[:, 0:3],
                                                     tailq[:, m, :]))
                    tv1 = vop(lambda v, b=b: v.tensor_copy(convst[:, 3:515],
                                                           pap(b)))
                    pdone(b, tv1)
                    vop(lambda v, m=m: v.tensor_copy(tailq[:, m, :],
                                                     convst[:, 512:515]))
                    vop(lambda v, m=m: v.tensor_scalar(
                        wC[:, 0:512], convst[:, 0:512], conv_s[:, m, 0:1],
                        None, ALU.mult))
                    vop(lambda v, m=m: v.scalar_tensor_tensor(
                        wC[:, 0:512], convst[:, 1:513], conv_s[:, m, 1:2],
                        wC[:, 0:512], ALU.mult, ALU.add))
                    vop(lambda v, m=m: v.scalar_tensor_tensor(
                        wC[:, 0:512], convst[:, 2:514], conv_s[:, m, 2:3],
                        wC[:, 0:512], ALU.mult, ALU.add))
                    tv2 = vop(lambda v, m=m: v.scalar_tensor_tensor(
                        wC[:, 0:512], convst[:, 3:515], conv_s[:, m, 3:4],
                        wC[:, 0:512], ALU.mult, ALU.add))
                    p.wait("scalar", tv2)
                    ts1 = sop(lambda s: s.activation(wD[:, 0:512],
                                                     wC[:, 0:512], AF.Silu))
                    p.wait("sync", ts1)
                    if m < 4:
                        dst = q_scr[m * 128:(m + 1) * 128,
                                    TB * 512:(TB + 1) * 512]
                    elif m < 8:
                        dst = k_scr[(m - 4) * 128:(m - 3) * 128,
                                    TB * 512:(TB + 1) * 512]
                    else:
                        dst = v_scr[(m - 8) * 128:(m - 7) * 128,
                                    TB * 512:(TB + 1) * 512]
                    tks = p.dma("sync", dst, wD[:, 0:512])
                    p.wait("scalar", tks)
                    if m < 8:
                        ts2 = sop(lambda s: s.activation(
                            convst[:, 0:512], wD[:, 0:512], AF.Square))
                        if m % 4 == 0:
                            rsqbank[0] = pacq()
                        p.wait("tensor", ts2)
                        mm(rsqbank[0],
                           esq_s[:, (m % 4) * 16:(m % 4) * 16 + 16],
                           convst[:, 0:512], m % 4 == 0, m % 4 == 3, rows=16)
                        if m % 4 == 3:
                            p.wait_cur("scalar", "eT")
                            bnk = rsqbank[0]
                            sop(lambda s, b=bnk: s.activation(
                                rowt[0:16, 0:512], pap(b, 16, 512), AF.Ln,
                                bias=cc(CR_E6, 16)))
                            if m < 4:
                                s4 = sop(lambda s, TB=TB: s.activation(
                                    rsqq_s[0:16, TB * 512:(TB + 1) * 512],
                                    rowt[0:16, 0:512], AF.Exp, scale=-0.5,
                                    bias=cc(CR_L8, 16)))
                            else:
                                s4 = sop(lambda s, TB=TB: s.activation(
                                    rsqk_s[0:16, TB * 512:(TB + 1) * 512],
                                    rowt[0:16, 0:512], AF.Exp, scale=-0.5))
                            pdone(bnk, s4)

                p.wait_cur("scalar", "qS")
                p.wait_cur("vector", "qS")
                bb_ = pacq()
                bg_ = pacq()
                for k in range(16):
                    mm(bb_, wabT_s[:, k, 0:16], x1TBbuf[:, k, :], k == 0,
                       k == 15, rows=16)
                    mm(bg_, wabT_s[:, k, 16:32], x1TBbuf[:, k, :], k == 0,
                       k == 15, rows=16)
                p.wait_cur("scalar", "eT")
                p.wait_cur("vector", "eT")
                s1 = sop(lambda s, b=bb_: s.activation(
                    rt_[0:16, RW_B, :], pap(b, 16, 512), AF.Sigmoid))
                s2 = sop(lambda s, b=bb_: s.activation(
                    rowt[0:16, 512:1024], pap(b, 16, 512), AF.Exp,
                    scale=-1.0))
                pdone(bb_, s2)
                s3 = sop(lambda s: s.activation(
                    rowt[0:16, 512:1024], rowt[0:16, 512:1024], AF.Ln,
                    bias=cc(CR_ONEC, 16)))
                s4 = sop(lambda s, b=bg_: s.activation(
                    rowt[0:16, 1024:1536], pap(b, 16, 512), AF.Exp,
                    bias=dtal_s[:, 0:1]))
                pdone(bg_, s4)
                s5 = sop(lambda s: s.activation(
                    rowt[0:16, 1024:1536], rowt[0:16, 1024:1536], AF.Ln,
                    bias=cc(CR_ONEC, 16)))
                p.wait("vector", s3)
                p.wait("vector", s5)
                vop(lambda v: v.tensor_scalar(
                    rowt[0:16, 1024:1536], rowt[0:16, 1024:1536], negea,
                    None, ALU.mult))
                for c4 in range(4):
                    vop(lambda v, c4=c4: v.tensor_tensor_scan(
                        rt_[0:16, RW_L, c4 * 128:(c4 + 1) * 128],
                        ones16[:, 0:128],
                        rowt[0:16, 1024 + c4 * 128:1024 + (c4 + 1) * 128],
                        0.0, ALU.mult, ALU.add))
                p.fence("vector")
                tv = vop(lambda v: v.tensor_tensor(
                    rt_[0:16, RW_LB, :], rt_[0:16, RW_L, :],
                    rowt[0:16, 512:1024], ALU.subtract))
                p.wait("scalar", tv)
                s6 = sop(lambda s: s.activation(rt_[0:16, RW_EB, :],
                                                rt_[0:16, RW_L, :], AF.Exp))
                for c4 in range(4):
                    tvv = vop(lambda v, c4=c4: v.tensor_scalar(
                        rowt[0:16, 1536 + c4 * 128:1536 + (c4 + 1) * 128],
                        rt_[0:16, RW_L, c4 * 128:(c4 + 1) * 128],
                        rt_[0:16, RW_L, c4 * 128 + 127:c4 * 128 + 128],
                        None, ALU.subtract))
                p.wait("scalar", tvv)
                s7 = sop(lambda s: s.activation(
                    rt_[0:16, RW_EXL, :], rowt[0:16, 1536:2048], AF.Exp,
                    scale=-1.0))
                p.wait("vector", s6)
                p.wait("vector", s7)
                vop(lambda v, TB=TB: v.tensor_copy(
                    rt_[0:16, RW_RQ, :],
                    rsqq_s[0:16, TB * 512:(TB + 1) * 512]))
                vop(lambda v, TB=TB: v.tensor_copy(
                    rt_[0:16, RW_RK, :],
                    rsqk_s[0:16, TB * 512:(TB + 1) * 512]))
                vop(lambda v: v.tensor_tensor(
                    rt_[0:16, RW_RQEB, :], rt_[0:16, RW_RQ, :],
                    rt_[0:16, RW_EB, :], ALU.mult))
                vop(lambda v: v.tensor_tensor(
                    rt_[0:16, RW_RKEB, :], rt_[0:16, RW_RK, :],
                    rt_[0:16, RW_EB, :], ALU.mult))
                tv = vop(lambda v: v.tensor_tensor(
                    rt_[0:16, RW_RKEXL, :], rt_[0:16, RW_RK, :],
                    rt_[0:16, RW_EXL, :], ALU.mult))
                p.wait_cur("sync", "eS", "eV")
                for r in range(NROWS):
                    p.dma("sync", rows_scr[:, r, TB * 512:(TB + 1) * 512],
                          rt_[0:16, r, :])

                for t4 in range(4):
                    tt = TB * 4 + t4
                    b = pacq()
                    for k in range(16):
                        mm(b, x1TBbuf[:, k, t4 * 128:(t4 + 1) * 128],
                           gwT_s[:, k, :], k == 0, k == 15, cols=8)
                    p.wait_cur("vector", "eT")
                    tv = vop(lambda v, b=b: v.tensor_scalar(
                        logit, pap(b, 128, 8), -50.0, 50.0, ALU.max, ALU.min))
                    pdone(b, tv)
                    vop(lambda v: v.tensor_reduce(m1c, logit, AXX, ALU.max))
                    p.fence("vector")
                    vop(lambda v: v.tensor_scalar(lmask, logit, m1c, None,
                                                  ALU.is_equal))
                    vop(lambda v: v.select(e1t, lmask, negbig_s, logit))
                    tv = vop(lambda v: v.tensor_reduce(m2c, e1t, AXX,
                                                       ALU.max))
                    p.fence("vector")
                    tv2 = vop(lambda v: v.tensor_scalar(rsqc, m1c, -1.0,
                                                        None, ALU.mult))
                    p.wait("scalar", tv)
                    p.wait("scalar", tv2)
                    se = sop(lambda s: s.activation(e1t, logit, AF.Exp,
                                                    bias=rsqc))
                    sg_ = sop(lambda s: s.activation(sgt, logit, AF.Sign,
                                                     scale=-1.0, bias=m2c))
                    tv = vop(lambda v: v.tensor_tensor(sscol, m2c, m1c,
                                                       ALU.subtract))
                    p.wait("scalar", tv)
                    sop(lambda s: s.activation(sscol, sscol, AF.Exp))
                    sop(lambda s: s.activation(sscol, sscol, AF.Ln,
                                               bias=cc(CR_ONEC)))
                    s10 = sop(lambda s: s.activation(rsqc, sscol, AF.Exp,
                                                     scale=-1.0))
                    p.wait("vector", se)
                    p.wait("vector", sg_)
                    p.wait("vector", s10)
                    vop(lambda v: v.scalar_tensor_tensor(
                        Ptile, sgt, 0.0, e1t, ALU.max, ALU.mult))
                    vop(lambda v: v.tensor_tensor(Ptile, e1t, Ptile,
                                                  ALU.subtract))
                    tv = vop(lambda v: v.tensor_scalar(Ptile, Ptile, rsqc,
                                                       None, ALU.mult))
                    b2 = pacq()
                    p.wait("tensor", tv)
                    ttk = tr(b2, Ptile, 128, mrows=8)
                    p.wait("vector", ttk)
                    tvv = vop(lambda v, b=b2: v.tensor_copy(
                        ptTB[0:8, 0:128], pap(b, 8, 128)))
                    pdone(b2, tvv)
                    p.wait("sync", tvv)
                    tkp = p.dma("sync", pt_scr[:, tt * 128:(tt + 1) * 128],
                                ptTB[0:8, 0:128])
                    p.wait("vector", tkp)
            p.barrier()

            ztok = {}
            for TB in range(4):
                for t4 in range(4):
                    tt = TB * 4 + t4
                    bi = t4 % 2
                    p.wait("sync", xg1[bi])
                    p.wait_cur("sync", "eT")
                    tkx = p.dma("sync", xb[bi],
                                x1_scr[tt * 128:(tt + 1) * 128, :])
                    p.wait("vector", tkx)
                    p.wait_cur("vector", "eT", "eS")
                    ts1 = vop(lambda v, bi=bi, tt=tt: v.tensor_scalar(
                        xn_t, xb[bi], nmr1_s[:, tt:tt + 1],
                        rstd1_s[:, tt:tt + 1], ALU.add, ALU.mult))
                    xg1[bi] = ts1
                    p.wait("tensor", ts1)
                    for k in range(16):
                        b2 = pacq()
                        ttk = tr(b2, xn_t[:, k * 128:(k + 1) * 128], 128)
                        p.wait("vector", ttk)
                        tvv = vop(lambda v, b=b2, k=k, t4=t4: v.tensor_copy(
                            x1TBbuf[:, k, t4 * 128:(t4 + 1) * 128],
                            pap(b, 128, 128)))
                        pdone(b2, tvv)
                for zc in range(2):
                    for kh in range(2):
                        p.wait_cur("sync", "eT")
                        for k8 in range(8):
                            k = kh * 8 + k8
                            p.dma("sync", wzTh[:, k8, :],
                                  wzT_scr[k * 128:(k + 1) * 128,
                                          zc * 512:(zc + 1) * 512])
                        p.wait_cur("tensor", "qS")
                        p.wait_cur("tensor", "eV")
                        for t4 in range(4):
                            tt = TB * 4 + t4
                            b = pacq()
                            for k8 in range(8):
                                mm(b, x1TBbuf[:, kh * 8 + k8,
                                              t4 * 128:(t4 + 1) * 128],
                                   wzTh[:, k8, :], k8 == 0, k8 == 7)
                            p.wait_cur("vector", "eT")
                            if kh == 0:
                                tv = vop(lambda v, b=b: v.tensor_copy(
                                    wC[:, 0:512], pap(b)))
                                pdone(b, tv)
                                p.wait("sync", tv)
                                tkz = p.dma(
                                    "sync",
                                    z_scr[tt * 128:(tt + 1) * 128,
                                          zc * 512:(zc + 1) * 512],
                                    wC[:, 0:512])
                                ztok[(tt, zc)] = tkz
                                p.wait("vector", tkz)
                            else:
                                p.wait("scalar", ztok[(tt, zc)])
                                tkz = p.dma("scalar", wD[:, 0:512],
                                            z_scr[tt * 128:(tt + 1) * 128,
                                                  zc * 512:(zc + 1) * 512])
                                p.wait("vector", tkz)
                                tv = vop(lambda v, b=b: v.tensor_tensor(
                                    wC[:, 0:512], pap(b), wD[:, 0:512],
                                    ALU.add))
                                pdone(b, tv)
                                p.wait("scalar", tv)
                                tsz = sop(lambda s: s.activation(
                                    wC[:, 0:512], wC[:, 0:512], AF.Silu))
                                p.wait("sync", tsz)
                                tkz2 = p.dma(
                                    "sync",
                                    z_scr[tt * 128:(tt + 1) * 128,
                                          zc * 512:(zc + 1) * 512],
                                    wC[:, 0:512])
                                p.wait("vector", tkz2)
                                p.wait("scalar", tkz2)
            p.barrier()

        if upto >= 7:
            p.dma("sync", eh128_s, eh128_d)
            p.dma("sync", ehn128_s, ehn128_d)
            p.dma("sync", eh64_s, eh64_d)
            gop(lambda g: g.memset(Ssta, 0.0))
            p.barrier()

            for c in range(16):
                p.wait_cur("sync", "eV", "eT", "eS")
                tkq = p.dma("sync", qc_sb,
                            q_scr[:, c * 128:(c + 1) * 128].rearrange(
                                "(a p) t -> p a t", p=64))
                tkk = p.dma("sync", kc_sb,
                            k_scr[:, c * 128:(c + 1) * 128].rearrange(
                                "(a p) t -> p a t", p=64))
                tkv = p.dma("sync", vc_sb,
                            v_scr[:, c * 128:(c + 1) * 128].rearrange(
                                "(a p) t -> p a t", p=64))
                tkz = p.dma("sync", zc_sb, z_scr[c * 128:(c + 1) * 128, :])
                tkr = p.dma("sync", rowchk,
                            rows_scr[:, 0:NROWS, c * 128:(c + 1) * 128])
                p.wait("vector", tkz)
                p.wait("tensor", tkr)
                for rsl, dstc in ((RW_L, colsL), (RW_LB, colsLB),
                                  (RW_B, colsB), (RW_RQEB, colsRQE),
                                  (RW_RKEB, colsRKE), (RW_RKEXL, colsRKX),
                                  (RW_RK, colsRK)):
                    b2 = pacq()
                    ttk = tr(b2, rowchk[:, rsl, :], 16)
                    p.wait("vector", ttk)
                    tvv = vop(lambda v, b=b2, d=dstc: v.tensor_copy(
                        d, pap(b, 128, 16)))
                    pdone(b2, tvv)
                p.wait("tensor", tkq)
                p.wait("tensor", tkk)
                p.wait("tensor", tkv)
                p.wait_cur("tensor", "eV")

                for h in range(16):
                    qh = h // 2
                    qsl = qc_sb[:, qh, :]
                    ksl = kc_sb[:, qh, :]
                    vsl = vc_sb[:, h, :]
                    Sh = Ssta[:, h, :]
                    bA = pacq()
                    t1 = mm(bA, eh128_s[:, h * 128:(h + 1) * 128],
                            rowchk[:, RW_LB, :], True, True, cols=128)
                    bB = pacq()
                    t2 = mm(bB, eh128_s[:, h * 128:(h + 1) * 128],
                            rowchk[:, RW_L, :], True, True, cols=128)
                    bC2 = pacq()
                    t3 = mm(bC2, ehn128_s[:, h * 128:(h + 1) * 128],
                            rowchk[:, RW_L, :], True, True, cols=128)
                    bRK = pacq()
                    t6 = mm(bRK, eh128_s[:, h * 128:(h + 1) * 128],
                            rowchk[:, RW_RK, :], True, True, cols=128)
                    bRQ = pacq()
                    t7 = mm(bRQ, eh128_s[:, h * 128:(h + 1) * 128],
                            rowchk[:, RW_RQ, :], True, True, cols=128)
                    bG1 = pacq()
                    t4_ = mm(bG1, ksl, ksl, True, True, cols=128)
                    bG2 = pacq()
                    t5 = mm(bG2, ksl, qsl, True, True, cols=128)
                    p.wait("vector", t1)
                    tv = vop(lambda v, b=bA, h=h: v.tensor_scalar(
                        E1, pap(b, 128, 128), colsL[:, h:h + 1], 0.0,
                        ALU.subtract, ALU.min))
                    pdone(bA, tv)
                    p.wait("scalar", tv)
                    se1 = sop(lambda s: s.activation(E1, E1, AF.Exp))
                    p.wait("vector", t3)
                    tv = vop(lambda v, b=bC2, h=h: v.tensor_scalar(
                        E1b, pap(b, 128, 128), colsLB[:, h:h + 1], 0.0,
                        ALU.add, ALU.min))
                    pdone(bC2, tv)
                    p.wait("scalar", tv)
                    se2 = sop(lambda s: s.activation(E1b, E1b, AF.Exp))
                    p.wait("vector", t2)
                    tv = vop(lambda v, b=bB, h=h: v.tensor_scalar(
                        E2, pap(b, 128, 128), colsL[:, h:h + 1], 0.0,
                        ALU.subtract, ALU.min))
                    pdone(bB, tv)
                    p.wait("scalar", tv)
                    se3 = sop(lambda s: s.activation(E2, E2, AF.Exp))
                    p.wait("vector", t6)
                    tv0 = vop(lambda v, bk=bRK: v.tensor_copy(
                        rkb, pap(bk, 128, 128)))
                    pdone(bRK, tv0)
                    p.wait("vector", t7)
                    tv0 = vop(lambda v, bq=bRQ: v.tensor_copy(
                        rqb, pap(bq, 128, 128)))
                    pdone(bRQ, tv0)
                    p.wait("vector", t4_)
                    tv = vop(lambda v, bg=bG1, h=h:
                             v.scalar_tensor_tensor(
                                 Gbar, pap(bg, 128, 128), colsRK[:, h:h + 1],
                                 rkb, ALU.mult, ALU.mult))
                    pdone(bG1, tv)
                    p.wait("vector", t5)
                    tv = vop(lambda v, bg=bG2, h=h:
                             v.scalar_tensor_tensor(
                                 Gqbar, pap(bg, 128, 128), colsRK[:, h:h + 1],
                                 rqb, ALU.mult, ALU.mult))
                    pdone(bG2, tv)
                    p.wait("vector", se1)
                    p.wait("vector", se2)
                    p.wait("vector", se3)
                    vop(lambda v: v.tensor_tensor(P0b, E1, Gbar, ALU.mult))
                    vop(lambda v: v.tensor_tensor(P0b, P0b, mUneg, ALU.mult))
                    vop(lambda v: v.tensor_tensor(N0b, E1b, Gbar, ALU.mult))
                    vop(lambda v: v.tensor_tensor(N0b, N0b, mLneg, ALU.mult))
                    vop(lambda v: v.tensor_tensor(AqT, E2, Gqbar, ALU.mult))
                    vop(lambda v: v.tensor_tensor(AqT, AqT, mUinc, ALU.mult))
                    bV = pacq()
                    tvv = tr(bV, vsl, 64)
                    p.wait("vector", tvv)
                    tv = vop(lambda v, b=bV: v.tensor_copy(Vt,
                                                           pap(b, 128, 64)))
                    pdone(bV, tv)
                    bR = pacq()
                    trr = mm(bR, ksl, Sh, True, True, cols=64)
                    p.wait("vector", trr)
                    tv = vop(lambda v, b=bR, h=h: v.tensor_scalar(
                        Rt, pap(b, 128, 64), colsRKE[:, h:h + 1], None,
                        ALU.mult))
                    pdone(bR, tv)
                    vop(lambda v: v.tensor_tensor(Rt, Vt, Rt, ALU.subtract))
                    vop(lambda v, h=h: v.tensor_scalar(
                        Rt, Rt, colsB[:, h:h + 1], None, ALU.mult))
                    vop(lambda v: v.tensor_copy(Xa, Rt))
                    vop(lambda v: v.tensor_copy(Pk, P0b))
                    tv = vop(lambda v: v.tensor_copy(Nk, N0b))
                    cur, nxt = Xa, Xb
                    for lev in range(7):
                        bX = pacq()
                        p.wait_cur("tensor", "eV")
                        tmm = mm(bX, Pk, cur, True, True, cols=64)
                        p.wait("vector", tmm)
                        tv = vop(lambda v, b=bX, cu=cur, nx=nxt:
                                 v.tensor_tensor(nx, cu, pap(b, 128, 64),
                                                 ALU.add))
                        pdone(bX, tv)
                        cur, nxt = nxt, cur
                        if lev < 6:
                            bN = pacq()
                            mm(bN, Pk, Nk, True, True, cols=128)
                            bP = pacq()
                            tm2 = mm(bP, Nk, Pk, True, True, cols=128)
                            p.wait("vector", tm2)
                            tv1 = vop(lambda v, b=bN: v.tensor_copy(
                                Nk, pap(b, 128, 128)))
                            pdone(bN, tv1)
                            tv2 = vop(lambda v, b=bP: v.tensor_copy(
                                Pk, pap(b, 128, 128)))
                            pdone(bP, tv2)
                    W_ = cur
                    bQ = pacq()
                    p.wait_cur("tensor", "eV")
                    tq1 = mm(bQ, qsl, Sh, True, True, cols=64)
                    bAq = pacq()
                    tq2 = mm(bAq, AqT, W_, True, True, cols=64)
                    p.wait("vector", tq2)
                    tv0 = vop(lambda v, b2=bAq: v.tensor_copy(
                        aqb[:, 0:64], pap(b2, 128, 64)))
                    pdone(bAq, tv0)
                    p.wait("vector", tq1)
                    tv = vop(lambda v, b1=bQ, h=h:
                             v.scalar_tensor_tensor(
                                 osb, pap(b1, 128, 64), colsRQE[:, h:h + 1],
                                 aqb[:, 0:64], ALU.mult, ALU.add))
                    pdone(bQ, tv)
                    vop(lambda v: v.tensor_tensor(osq, osb, osb, ALU.mult))
                    tv = vop(lambda v: v.tensor_reduce(sscol, osq, AXX,
                                                       ALU.add))
                    p.wait("scalar", tv)
                    sop(lambda s: s.activation(sscol, sscol, AF.Ln,
                                               bias=cc(CR_E6),
                                               scale=1.0 / 64.0))
                    s2 = sop(lambda s: s.activation(rsqc, sscol, AF.Exp,
                                                    scale=-0.5))
                    p.wait("vector", s2)
                    vop(lambda v: v.tensor_scalar(osb, osb, rsqc, None,
                                                  ALU.mult))
                    vop(lambda v, h=h: v.tensor_tensor(
                        ynstage[:, h * 64:(h + 1) * 64], osb,
                        zc_sb[:, h * 64:(h + 1) * 64], ALU.mult))
                    bK = pacq()
                    tkk2 = tr(bK, ksl, 64)
                    p.wait("vector", tkk2)
                    tv = vop(lambda v, b=bK, h=h: v.tensor_scalar(
                        Kt, pap(b, 128, 64), colsRKX[:, h:h + 1], None,
                        ALU.mult))
                    pdone(bK, tv)
                    bbc = pacq()
                    tb1 = mm(bbc, eh64_s[:, h * 64:(h + 1) * 64],
                             rowchk[:, RW_L, 127:128], True, True,
                             rows=64, cols=1)
                    p.wait("scalar", tb1)
                    sb1 = sop(lambda s, b=bbc: s.activation(
                        bCc, pap(b, 64, 1), AF.Exp))
                    pdone(bbc, sb1)
                    p.wait("vector", sb1)
                    tv = vop(lambda v: v.tensor_scalar(bCI, id64_s, bCc,
                                                       None, ALU.mult))
                    bS = pacq()
                    p.wait_cur("tensor", "eV")
                    mm(bS, Kt, W_, True, False, rows=64, cols=64)
                    ts2 = mm(bS, bCI, Sh, False, True, rows=64, cols=64)
                    p.wait("vector", ts2)
                    tv = vop(lambda v, b=bS, Sh=Sh: v.tensor_copy(
                        Sh, pap(b, 64, 64)))
                    pdone(bS, tv)
                p.wait_cur("tensor", "eV")
                for pr in range(8):
                    b2 = pacq()
                    ttk = tr(b2, ynstage[:, pr * 128:(pr + 1) * 128], 128)
                    p.wait("vector", ttk)
                    tvv = vop(lambda v, b=b2, pr=pr, c=c: v.tensor_copy(
                        ynzT[:, pr, c * 128:(c + 1) * 128], pap(b, 128, 128)))
                    pdone(b2, tvv)
            p.barrier()

        if upto >= 8:
            for o in range(4):
                p.wait_cur("sync", "eT", "eV")
                for k in range(16):
                    p.dma("sync", wTblk[:, k, :],
                          wT_scr[k * 128:(k + 1) * 128,
                                 o * 512:(o + 1) * 512])
                for kv in range(8):
                    p.dma("sync", woutTblk[:, kv, :],
                          woutT_scr[kv * 128:(kv + 1) * 128,
                                    o * 512:(o + 1) * 512])
                p.wait_cur("tensor", "qS")
                p.wait_cur("vector", "qS")
                for TB in range(4):
                    p.wait_cur("sync", "eT")
                    tkp = p.dma("sync", ptTB,
                                pt_scr[:, TB * 512:(TB + 1) * 512])
                    for t4 in range(4):
                        vop(lambda v, t4=t4: v.memset(contribTB[:, t4, :],
                                                      0.0))
                    p.wait("tensor", tkp)
                    segg = [None, None]
                    tmv_prev = [None, None]
                    for ib, (bck, wlo, whi) in enumerate(meta["o_chunks"][o]):
                        wwid = whi - wlo + 1
                        sbi = ib % 2
                        p.wait("sync", segg[sbi])
                        tks = p.dma("sync", segbuf[:, sbi, :],
                                    seg_d[bck * 128:(bck + 1) * 128, :])
                        p.wait("scalar", segg[sbi])
                        p.wait("scalar", tmv_prev[sbi])
                        tkw = p.dma("scalar", wpbuf[:, sbi, :],
                                    wp_scr[:, bck * 128:(bck + 1) * 128])
                        p.wait_cur("gpsimd", "eV")
                        tgk = p.gather(xgT, xn1T_scr, gidx_s[:, bck:bck + 1],
                                       element_offset=TB * 512)
                        bv = pacq()
                        p.wait("tensor", tkw)
                        tmv = mm(bv, wpbuf[0:8, sbi, :], ptTB[0:8, :],
                                 True, True)
                        tmv_prev[sbi] = tmv
                        p.wait("vector", tmv)
                        p.wait("vector", tgk)
                        tvp = vop(lambda v, b=bv: v.tensor_tensor(
                            prodb, pap(b), xgT, ALU.mult))
                        pdone(bv, tvp)
                        p.wait("tensor", tvp)
                        p.wait("tensor", tks)
                        woff = wlo - meta["chunk_rlo"][bck]
                        tsm = None
                        for t4 in range(4):
                            b2 = pacq()
                            tsm = mm(b2, prodb[:, t4 * 128:(t4 + 1) * 128],
                                     segbuf[:, sbi, woff:woff + wwid],
                                     True, True, cols=wwid)
                            p.wait("vector", tsm)
                            tvv = vop(lambda v, b=b2, t4=t4, wlo=wlo, o=o,
                                      ww=wwid: v.tensor_tensor(
                                          contribTB[:, t4, wlo - o * 512:
                                                    wlo - o * 512 + ww],
                                          contribTB[:, t4, wlo - o * 512:
                                                    wlo - o * 512 + ww],
                                          pap(b, 128, ww), ALU.add))
                            pdone(b2, tvv)
                        segg[sbi] = tsm
                    xng = [None, None]
                    for t4 in range(4):
                        tt = TB * 4 + t4
                        bi = t4 % 2
                        p.wait("sync", xng[bi])
                        tkx = p.dma("sync",
                                    xn1Ttile[:, bi, :].rearrange(
                                        "p (k t) -> p k t", k=16),
                                    xn1T_scr[:, tt * 128:(tt + 1) * 128]
                                    .rearrange("(k p) t -> p k t", p=128))
                        p.wait("tensor", tkx)
                        b = pacq()
                        xv = xn1Ttile[:, bi, :].rearrange(
                            "p (k t) -> p k t", k=16)
                        lst = None
                        for k in range(16):
                            lst = mm(b, xv[:, k, :], wTblk[:, k, :],
                                     k == 0, k == 15)
                        xng[bi] = lst
                        p.wait_cur("vector", "eT")
                        tv = vop(lambda v, b=b, t4=t4: v.tensor_tensor(
                            t512a, pap(b), contribTB[:, t4, :], ALU.add))
                        pdone(b, tv)
                        p.wait("scalar", tv)
                        p.wait_cur("scalar", "eV")
                        sg1 = sop(lambda s: s.activation(g1sl, t512a,
                                                         AF.Silu))
                        b2 = pacq()
                        for kv in range(8):
                            mm(b2, ynzT[:, kv, tt * 128:(tt + 1) * 128],
                               woutTblk[:, kv, :], kv == 0, kv == 7)
                        p.wait_cur("scalar", "eV")
                        tkx1 = p.dma("scalar", x1sl,
                                     x1_scr[tt * 128:(tt + 1) * 128,
                                            o * 512:(o + 1) * 512])
                        p.wait("vector", sg1)
                        p.wait("vector", tkx1)
                        p.wait_cur("vector", "eT")
                        tv = vop(lambda v, b=b2: v.tensor_tensor(
                            t512b, pap(b), g1sl, ALU.mult))
                        pdone(b2, tv)
                        tv = vop(lambda v: v.tensor_tensor(t512b, t512b,
                                                           x1sl, ALU.add))
                        p.wait("sync", tv)
                        tko = p.dma("sync",
                                    out_d[tt * 128:(tt + 1) * 128,
                                          o * 512:(o + 1) * 512], t512b)
                        p.wait("vector", tko)

        p.barrier()
        if upto < 8:
            srcs = {"x1": x1_scr, "g": g_scr, "x": x_in}
            src = srcs.get(dump or ("x1" if upto >= 5 else "x"), x1_scr)
            if dump == "h":
                hv = h_scr.rearrange("a p f -> (a p) f")
                for tt in range(NT):
                    p.dma("sync", out_d[tt * 128:(tt + 1) * 128, :],
                          hv[tt * 128:(tt + 1) * 128, 0:D])
            elif dump == "dbg1":
                p.dma("sync", out_d[0:128, 0:NT], s1sum)
                p.dma("sync", out_d[128:256, 0:NT], s1ss)
                p.dma("sync", out_d[256:384, 0:NT], qsc_s)
                p.dma("sync", out_d[384:512, 0:NT], rstd2_s)
                p.dma("sync", out_d[512:640, 0:40], statc)
                p.dma("sync", out_d[640:768, 0:8], crow_s)
            elif dump == "misc":
                tvq = vop(lambda v: v.tensor_copy(xw0, yfm))
                p.wait("sync", tvq)
                p.dma("sync", out_d[0:128, :], xw0)
                for nm, tns in (("qsc", qsc_s), ("invq", invq_s),
                                ("rstd2", rstd2_s), ("nmr2", nmr2_s)):
                    pass
                p.dma("sync", out_d[128:256, 0:NT], qsc_s)
                p.dma("sync", out_d[256:384, 0:NT], invq_s)
                p.dma("sync", out_d[384:512, 0:NT], rstd2_s)
                p.dma("sync", out_d[512:640, 0:48], stats)
                p.dma("sync", out_d[640:768, 0:NT], comb1_s)
                p.dma("sync", out_d[768:896, 0:4], swcol)
            elif dump == "xq":
                for k in range(16):
                    for half in range(2):
                        tvq = vop(lambda v, k=k, half=half: v.tensor_copy(
                            xw0[:, 0:1024],
                            xqT[:, k, half * 1024:(half + 1) * 1024]))
                        p.wait("sync", tvq)
                        tkq = p.dma("sync",
                                    out_d[k * 128:(k + 1) * 128,
                                          half * 1024:(half + 1) * 1024],
                                    xw0[:, 0:1024])
                        p.wait("vector", tkq)
            else:
                for tt in range(NT):
                    p.dma("sync", out_d[tt * 128:(tt + 1) * 128, :],
                          src[tt * 128:(tt + 1) * 128, :])
        p.wait_cur("sync", "qS", "qA", "qG")
        p.wait_cur("gpsimd", "qS", "qA", "qG")

        @block.sync
        def _(e):
            for fn in p.prog["sync"]:
                fn(e)

        @block.vector
        def _(e):
            for fn in p.prog["vector"]:
                fn(e)

        @block.scalar
        def _(e):
            for fn in p.prog["scalar"]:
                fn(e)

        @block.tensor
        def _(e):
            for fn in p.prog["tensor"]:
                fn(e)

        @block.gpsimd
        def _(e):
            for fn in p.prog["gpsimd"]:
                fn(e)

    return nc


def _host_meta(mask_idx):
    rows = mask_idx // D
    cols = mask_idx % D
    order = np.argsort(rows, kind="stable")
    rs = rows[order].astype(np.int64)
    cs = cols[order].astype(np.int64)
    chunk_rlo, chunk_rhi = [], []
    for b in range(NCHK):
        seg = rs[b * 128:(b + 1) * 128]
        chunk_rlo.append(int(seg.min()))
        chunk_rhi.append(int(seg.max()))
    WSEG = max(hi - lo + 1 for lo, hi in zip(chunk_rlo, chunk_rhi))
    WSEG = int(np.ceil(WSEG / 8) * 8)
    seg_np = np.zeros((NCHK, 128, WSEG), dtype=np.float32)
    for b in range(NCHK):
        for ps in range(128):
            seg_np[b, ps, int(rs[b * 128 + ps] - chunk_rlo[b])] = 1.0
    seg_np = seg_np.reshape(NCHK * 128, WSEG)
    o_chunks = [[] for _ in range(4)]
    for b in range(NCHK):
        lo, hi = chunk_rlo[b], chunk_rhi[b]
        for o in range(4):
            wlo = max(lo, o * 512)
            whi = min(hi, o * 512 + 511)
            if wlo <= whi:
                o_chunks[o].append((b, wlo, whi))
    gidx = np.zeros((128, NCHK), dtype=np.int32)
    for b in range(NCHK):
        gidx[:, b] = cs[b * 128:(b + 1) * 128]
    return {"order": order, "WSEG": WSEG, "seg": seg_np,
            "o_chunks": o_chunks, "chunk_rlo": chunk_rlo, "gidx": gidx}


def _host_consts():
    import ml_dtypes
    c = {}
    c["identf"] = np.eye(128, dtype=np.float32)
    c["identb"] = np.eye(128, dtype=np.float32).astype(ml_dtypes.bfloat16)
    eh = np.zeros((16, 16 * 128), dtype=np.float32)
    ehn = np.zeros((16, 16 * 128), dtype=np.float32)
    for h in range(16):
        eh[h, h * 128:(h + 1) * 128] = 1.0
        ehn[h, h * 128:(h + 1) * 128] = -1.0
    c["eh128"] = eh
    c["ehn128"] = ehn
    e64 = np.zeros((16, 16 * 64), dtype=np.float32)
    for h in range(16):
        e64[h, h * 64:(h + 1) * 64] = 1.0
    c["eh64"] = e64
    esq = np.zeros((128, 64), dtype=np.float32)
    for m4 in range(4):
        for r in range(128):
            qh = 2 * m4 + r // 64
            for j in (2 * qh, 2 * qh + 1):
                esq[r, m4 * 16 + j] = 1.0
    c["esq"] = esq
    c["maskUneg"] = -np.triu(np.ones((128, 128), dtype=np.float32), 1)
    c["maskLneg"] = -np.tril(np.ones((128, 128), dtype=np.float32), -1)
    c["maskUinc"] = np.triu(np.ones((128, 128), dtype=np.float32), 0)
    crow = np.zeros((128, 8), dtype=np.float32)
    crow[:, CR_NQ] = -0.25
    crow[:, CR_E6] = 1e-6
    crow[:, CR_E5] = 1e-5
    crow[:, CR_L127] = np.log(127.0)
    crow[:, CR_HALF] = 0.5
    crow[:, CR_ONEC] = 1.0
    crow[:, CR_L8] = np.log(0.125)
    c["crow"] = crow
    c["crow16"] = crow[0:16, :].copy()
    c["onescol"] = np.ones((128, 1), dtype=np.float32)
    c["onesrow"] = np.ones((1, 128), dtype=np.float32)
    c["allones"] = np.ones((128, 128), dtype=np.float32)
    c["id64"] = np.eye(64, dtype=np.float32)
    c["negbig"] = np.full((128, 8), -1e30, dtype=np.float32)
    return c


def kernel(**inp):
    meta = _host_meta(np.asarray(inp["L1_ffn_mask_idx"]).astype(np.int64))
    consts = _host_consts()
    nc = build(meta, upto=8)

    order = meta["order"]
    atoms_s = np.ascontiguousarray(np.asarray(inp["L1_ffn_atoms"])[:, order])
    imp_s = np.ascontiguousarray(
        np.asarray(inp["L1_ffn_importance"])[:, order])
    convw = np.asarray(inp["L1_attn_conv_w"]).reshape(D, 4)
    conv_sb = np.ascontiguousarray(
        convw.reshape(16, 128, 4).transpose(1, 0, 2))
    dtal = np.stack([np.asarray(inp["L1_attn_dt_bias"]),
                     np.asarray(inp["L1_attn_A_log"])], axis=1)
    dtal = np.ascontiguousarray(dtal.astype(np.float32))

    common = {
        "W_in": inp["L0_attn_W_in"], "W_out0": inp["L0_attn_W_out"],
        "W_skip": inp["L0_attn_W_skip"], "w1": inp["L0_ffn_w1"],
        "w2": inp["L0_ffn_w2"], "W_qkv": inp["L1_attn_W_qkv"],
        "W_z": inp["L1_attn_W_z"], "Wb": inp["L1_attn_W_b"],
        "Wa": inp["L1_attn_W_a"], "W_out1": inp["L1_attn_W_out"],
        "conv_w": conv_sb, "dtb_alog": dtal,
        "gate_w": inp["L1_ffn_gate_w"], "Wffn": inp["L1_ffn_W"],
        "atoms_s": atoms_s, "imp_s": imp_s,
        "eaw": inp["L1_ffn_expert_atom_weights"],
        "gidx": meta["gidx"], "seg": meta["seg"],
    }
    common.update(consts)
    common = {k: np.ascontiguousarray(np.asarray(v)) for k, v in
              common.items()}

    x = np.asarray(inp["L0_attn_x"])
    in_maps = []
    for core in range(8):
        m = dict(common)
        m["x"] = np.ascontiguousarray(x[core % 4])
        in_maps.append(m)

    res = run_bass_kernel_spmd(nc, in_maps, core_ids=list(range(8)))
    out = np.stack([res.results[i]["out"] for i in range(4)], axis=0)
    return out.astype(np.float32)
